# version 01
# baseline (speedup 1.0000x reference)
import contextlib
import numpy as np
import ml_dtypes
import concourse.bass as bass
import concourse.mybir as mybir
from concourse.bass_utils import run_bass_kernel_spmd

F32 = mybir.dt.float32
BF16 = mybir.dt.bfloat16
F16 = mybir.dt.float16
U8 = mybir.dt.uint8
AF = mybir.ActivationFunctionType
ALU = mybir.AluOpType
AX = mybir.AxisListType
BF = ml_dtypes.bfloat16

T = 2048
D = 2048
NT = T // 128
ND = D // 128
NHEAD = 16
NKV = 4
I_FFN = 5632
R_G = 2560
FF_MOE = 8192
EPS_LN = 1e-5
EPS_RMS = 1e-6
KB = 1024


def _perm_dnew():
    new = np.arange(D)
    return (8 * (new % 256) + new // 256).astype(np.int64)

def _perm_inew():
    new = np.arange(I_FFN)
    return (8 * (new % 704) + new // 704).astype(np.int64)

def _perm_rho(nhead):
    base = np.concatenate([np.arange(0, 128, 2), np.arange(1, 128, 2)])
    return np.concatenate([h * 128 + base for h in range(nhead)]).astype(np.int64)

ORIG_OF_NEW = _perm_dnew()
NEW_OF_ORIG = np.argsort(ORIG_OF_NEW)
I_ORIG_OF_NEW = _perm_inew()
RHO_Q = _perm_rho(NHEAD)
RHO_K = _perm_rho(NKV)


def _rope_tables():
    theta = 10000.0
    freqs = theta ** (-(np.arange(0, D, 2, dtype=np.float64) / D))[:64]
    ang = np.outer(np.arange(T, dtype=np.float64), freqs)
    cos = np.cos(ang).astype(np.float32)
    sin = np.sin(ang).astype(np.float32)
    return (np.ascontiguousarray(np.tile(cos, (1, NHEAD))),
            np.ascontiguousarray(np.tile(sin, (1, NHEAD))))


def _consts():
    c = np.zeros((128, 8), np.float32)
    c[:, 0] = EPS_LN
    c[:, 1] = EPS_RMS
    c[:, 2] = 1.0
    return c


def _rep128(v):
    return np.ascontiguousarray(np.repeat(np.asarray(v, np.float32)[None, :], 128, 0))



class Sem:
    __slots__ = ("h", "n", "name")
    def __init__(self, name):
        self.h = None
        self.n = 0
        self.name = name


class Ctx:
    ENGINES = ("sync", "tensor", "vector", "scalar", "gpsimd")

    def __init__(self, nc):
        self.nc = nc
        self.ops = {e: [] for e in self.ENGINES}
        self.sems = {}

    def sem(self, name):
        if name not in self.sems:
            self.sems[name] = Sem(name)
        return self.sems[name]

    def emit(self, eng, fn, waits=(), inc=None, inc_amt=1):
        waits = tuple((s, v) for (s, v) in waits if v > 0)
        def closure(e, fn=fn, waits=waits, inc=inc, inc_amt=inc_amt):
            for s, v in waits:
                e.wait_ge(s.h, v)
            ins = fn(e)
            if inc is not None:
                ins.then_inc(inc.h, inc_amt)
        self.ops[eng].append(closure)
        if inc is not None:
            inc.n += inc_amt
        return inc.n if inc is not None else None

    def finalize(self, block):
        for eng in self.ENGINES:
            getattr(block, eng)(lambda e, eng=eng: [f(e) for f in self.ops[eng]])


class Arena:
    def __init__(self, nc, nbytes, name="arena"):
        self.nc = nc
        self.backing = nc.alloc_sbuf_tensor(name, [128, nbytes], U8)
        self.base = nc.lookup_mloc(self.backing).addr
        self.nbytes = nbytes
        self._n = 0

    def at(self, name, off_bytes, shape, dtype):
        assert off_bytes + int(np.prod(shape[1:])) * mybir.dt.size(dtype) <= self.nbytes, \
            (name, off_bytes)
        self._n += 1
        v = self.nc.virtual_sbuf_tensor(f"{name}_{self._n}", list(shape), dtype)
        self.nc._tensor(v.name, list(v.shape), v.dtype, type="SB")
        mloc = self.nc.lookup_mloc(v)
        mloc.allocated = True
        mloc.addr = self.base + off_bytes
        return v


class Psum:
    def __init__(self, nc):
        self.nc = nc
        self._n = 0

    def at(self, name, bank, shape, dtype):
        assert bank * 2048 + int(np.prod(shape[1:])) * mybir.dt.size(dtype) <= 16384
        self._n += 1
        v = self.nc.virtual_psum_tensor(f"{name}_{self._n}", list(shape), dtype)
        self.nc._tensor(v.name, list(v.shape), v.dtype, type="PSUM")
        mloc = self.nc.lookup_mloc(v)
        mloc.allocated = True
        mloc.addr = bank * 2048
        return v


def build_and_run(build_fn, in_maps):
    nc = bass.Bass("TRN2", target_bir_lowering=False)
    ctx = Ctx(nc)
    build_fn(nc, ctx)
    with contextlib.ExitStack() as st:
        for name, s in ctx.sems.items():
            s.h = st.enter_context(nc.semaphore(name))
        block = st.enter_context(nc.Block())
        ctx.finalize(block)
    return run_bass_kernel_spmd(nc, in_maps, core_ids=list(range(len(in_maps))))



def build_launchA(nc, ctx):
    x = nc.dram_tensor("x", [T, D], F32, kind="ExternalInput").ap()
    wq = nc.dram_tensor("wq", [D, D], F32, kind="ExternalInput").ap()
    wk = nc.dram_tensor("wk", [D, 512], F32, kind="ExternalInput").ap()
    wv = nc.dram_tensor("wv", [D, 512], F32, kind="ExternalInput").ap()
    bq = nc.dram_tensor("bq", [128, D], BF16, kind="ExternalInput").ap()
    bk = nc.dram_tensor("bk", [128, 512], BF16, kind="ExternalInput").ap()
    bv = nc.dram_tensor("bv", [128, 512], BF16, kind="ExternalInput").ap()
    cosr = nc.dram_tensor("cosr", [T, 1024], F32, kind="ExternalInput").ap()
    sinr = nc.dram_tensor("sinr", [T, 1024], F32, kind="ExternalInput").ap()
    ident = nc.dram_tensor("ident", [128, 128], BF16, kind="ExternalInput").ap()
    cst = nc.dram_tensor("cst", [128, 8], F32, kind="ExternalInput").ap()

    o_xn0t = nc.dram_tensor("o_xn0t", [128, ND * T], BF16, kind="ExternalOutput").ap()
    o_qt = nc.dram_tensor("o_qt", [NT, NHEAD, 128, 128], BF16, kind="ExternalOutput").ap()
    o_kt = nc.dram_tensor("o_kt", [NT, NKV, 128, 128], BF16, kind="ExternalOutput").ap()
    o_v = nc.dram_tensor("o_v", [NT, 128, 512], BF16, kind="ExternalOutput").ap()

    ar = Arena(nc, 206 * KB)
    off = 0
    xn0t = ar.at("xn0t", off, [128, ND * T], BF16); off += 64 * KB
    wqs = ar.at("wqs", off, [128, ND * D], BF16); off += 64 * KB
    wks = ar.at("wks", off, [128, ND * 512], BF16); off += 16 * KB
    wvs = ar.at("wvs", off, [128, ND * 512], BF16); off += 16 * KB
    xst = [ar.at("xst0", off, [128, D], F32)] * 2; off += 8 * KB
    qf = ar.at("qf", off, [128, D], F32)
    kvf = ar.at("kvf", off, [128, 512], F32); off += 8 * KB
    scr = ar.at("scr", off, [128, 1024], BF16); off += 2 * KB
    tabs = ar.at("tabs", off, [128, 2048], F32); off += 8 * KB
    rq = ar.at("rq", off, [128, D], BF16); off += 4 * KB
    qstage = ar.at("qstage", off, [128, D], BF16); off += 4 * KB
    kstage = ar.at("kstage", off, [128, 512], BF16); off += 1 * KB
    vstage = ar.at("vstage", off, [128, 512], BF16); off += 1 * KB
    bqr = ar.at("bqr", off, [128, D], BF16); off += 4 * KB
    bkr = ar.at("bkr", off, [128, 512], BF16); off += 1 * KB
    bvr = ar.at("bvr", off, [128, 512], BF16); off += 1 * KB
    sid = ar.at("sid", off, [128, 128], BF16); off += 256
    stat = ar.at("stat", off, [128, 8], F32); off += 32
    scst = ar.at("scst", off, [128, 8], F32); off += 32
    assert off <= 206 * KB, off

    ps = Psum(nc)
    pq = ps.at("pq", 0, [128, 2048], F32)
    pkv = ps.at("pkv", 4, [128, 512], F32)
    ptr = ps.at("ptr", 5, [128, 1024], BF16)
    ptr2 = ps.at("ptr2", 6, [128, 1024], BF16)

    s_x = ctx.sem("s_x"); s_tab = ctx.sem("s_tab"); s_w = ctx.sem("s_w")
    s_ln = ctx.sem("s_ln"); s_tp = ctx.sem("s_tp"); s_ev = ctx.sem("s_ev")
    s_wu = ctx.sem("s_wu"); s_qe = ctx.sem("s_qe"); s_pre = ctx.sem("s_pre")
    s_st = ctx.sem("s_st")
    E = ctx.emit

    for src_, dst in ((ident, sid), (cst, scst), (bq, bqr), (bk, bkr), (bv, bvr)):
        E("sync", lambda e, src_=src_, dst=dst: e.dma_start(out=dst[:], in_=src_),
          inc=s_pre, inc_amt=16)
    for k in range(ND):
        E("gpsimd", lambda e, k=k: e.dma_start(out=wqs[:, k * D:(k + 1) * D],
                                               in_=wq[k * 128:(k + 1) * 128, :]),
          inc=s_w, inc_amt=16)
        E("gpsimd", lambda e, k=k: e.dma_start(out=wks[:, k * 512:(k + 1) * 512],
                                               in_=wk[k * 128:(k + 1) * 128, :]),
          inc=s_w, inc_amt=16)
        E("gpsimd", lambda e, k=k: e.dma_start(out=wvs[:, k * 512:(k + 1) * 512],
                                               in_=wv[k * 128:(k + 1) * 128, :]),
          inc=s_w, inc_amt=16)
    vwall = s_w.n
    vpre = s_pre.n

    def proj(m, ws, wcols, psum_t, first_waits):
        nch = wcols // 512
        for k in range(ND):
            for c in range(nch):
                E("tensor", lambda e, m=m, k=k, c=c, ws=ws, wcols=wcols, psum_t=psum_t:
                  e.matmul(out=psum_t[:, c * 512:(c + 1) * 512],
                           lhsT=xn0t[:, k * T + m * 128: k * T + (m + 1) * 128],
                           rhs=ws[:, k * wcols + c * 512: k * wcols + (c + 1) * 512],
                           start=(k == 0), stop=(k == ND - 1)),
                  waits=(first_waits if (k == 0 and c == 0) else []),
                  inc=(s_wu if (k == ND - 1 and c == nch - 1) else None))
        return s_wu.n

    def rope(src_f32, dst_bf16, nhead):
        w64 = nhead * 64
        def half(t, hoff):
            return t[:, 0:nhead * 128].rearrange("p (h two i) -> p h two i",
                                                 h=nhead, two=2)[:, :, hoff, :]
        def tab(off):
            return tabs[:, off:off + w64].rearrange("p (h i) -> p h i", h=nhead)
        def sc():
            return scr[:, 0:w64].rearrange("p (h i) -> p h i", h=nhead)
        E("vector", lambda e: e.tensor_tensor(out=sc(), in0=half(src_f32, 1),
                                              in1=tab(1024), op=ALU.mult))
        E("vector", lambda e: e.tensor_tensor(out=half(dst_bf16, 0), in0=half(src_f32, 0),
                                              in1=tab(0), op=ALU.mult))
        E("vector", lambda e: e.tensor_tensor(out=half(dst_bf16, 0), in0=half(dst_bf16, 0),
                                              in1=sc(), op=ALU.subtract))
        E("vector", lambda e: e.tensor_tensor(out=sc(), in0=half(src_f32, 1),
                                              in1=tab(0), op=ALU.mult))
        E("vector", lambda e: e.tensor_tensor(out=half(dst_bf16, 1), in0=half(src_f32, 0),
                                              in1=tab(1024), op=ALU.mult))
        return E("vector", lambda e: e.tensor_tensor(out=half(dst_bf16, 1),
                                                     in0=half(dst_bf16, 1),
                                                     in1=sc(), op=ALU.add), inc=s_ln)

    def transposes(n, src_bf16, dststage, first_wait):
        for half in range((n + 7) // 8):
            pt = ptr if half % 2 == 0 else ptr2
            cnt = min(8, n - half * 8)
            for kk in range(cnt):
                k = half * 8 + kk
                E("tensor", lambda e, k=k, kk=kk, pt=pt: e.transpose(
                    out=pt[:, kk * 128:(kk + 1) * 128],
                    in_=src_bf16[:, k * 128:(k + 1) * 128], identity=sid[:]),
                  waits=(first_wait + [(s_ev, s_ev.n)]) if kk == 0 else [],
                  inc=(s_tp if kk == cnt - 1 else None))
            vtp = s_tp.n
            E("vector", lambda e, half=half, pt=pt, cnt=cnt: e.tensor_copy(
                out=dststage[:, half * 1024: half * 1024 + cnt * 128],
                in_=pt[:, 0:cnt * 128]),
              waits=[(s_tp, vtp), (s_st, s_st.n)], inc=s_ev)
        return s_ev.n

    for m in range(NT):
        buf = m % 2
        E("sync", lambda e, m=m, buf=buf: e.dma_start(
            out=xst[buf][:], in_=x[m * 128:(m + 1) * 128, :]),
          waits=([(s_pre, vpre)] if m == 0 else []) + [(s_ln, max(0, 3 * (m - 1)))],
          inc=s_x, inc_amt=16)
        vx = s_x.n
        E("sync", lambda e, m=m: e.dma_start(out=tabs[:, 0:1024],
                                             in_=cosr[m * 128:(m + 1) * 128, :]),
          waits=[(s_ln, 3 * m)], inc=s_tab, inc_amt=16)
        E("sync", lambda e, m=m: e.dma_start(out=tabs[:, 1024:2048],
                                             in_=sinr[m * 128:(m + 1) * 128, :]),
          inc=s_tab, inc_amt=16)
        vtab = s_tab.n

        E("scalar", lambda e, buf=buf: e.activation(out=qf[:], in_=xst[buf][:],
                                                    func=AF.Copy, accum_out=stat[:, 2:3]),
          waits=[(s_x, vx), (s_qe, s_qe.n)])
        E("scalar", lambda e, buf=buf: e.activation(out=qf[:], in_=xst[buf][:],
                                                    func=AF.Square, accum_out=stat[:, 3:4]))
        E("scalar", lambda e: e.activation(out=stat[:, 0:1], in_=stat[:, 2:3],
                                           func=AF.Copy, scale=1.0 / D))
        E("scalar", lambda e: e.activation(out=stat[:, 4:5], in_=stat[:, 3:4],
                                           func=AF.Copy, scale=1.0 / D))
        vact = E("scalar", lambda e: e.activation(out=stat[:, 5:6], in_=stat[:, 0:1],
                                                  func=AF.Square), inc=s_ev)
        vv1 = E("vector", lambda e: e.tensor_tensor(out=stat[:, 6:7], in0=stat[:, 4:5],
                                                    in1=stat[:, 5:6], op=ALU.subtract),
                waits=[(s_ev, vact)], inc=s_ev)
        vs1 = E("scalar", lambda e: e.activation(out=stat[:, 7:8], in_=stat[:, 6:7],
                                                 func=AF.Sqrt, bias=scst[:, 0:1]),
                waits=[(s_ev, vv1)], inc=s_ev)
        E("vector", lambda e: e.reciprocal(out=stat[:, 1:2], in_=stat[:, 7:8]),
          waits=[(s_ev, vs1)])
        vln = E("vector", lambda e, buf=buf: e.tensor_scalar(
            out=rq[:], in0=xst[buf][:], scalar1=stat[:, 0:1], scalar2=stat[:, 1:2],
            op0=ALU.subtract, op1=ALU.mult), inc=s_ln)

        for half in range(2):
            pt = ptr if half == 0 else ptr2
            for kk in range(8):
                k = half * 8 + kk
                E("tensor", lambda e, k=k, kk=kk, pt=pt: e.transpose(
                    out=pt[:, kk * 128:(kk + 1) * 128], in_=rq[:, k * 128:(k + 1) * 128],
                    identity=sid[:]),
                  waits=([(s_ln, vln), (s_ev, s_ev.n)] +
                         ([(s_w, vwall)] if m == 0 and half == 0 else [])) if kk == 0 else [],
                  inc=(s_tp if kk == 7 else None))
            vtp = s_tp.n
            E("vector", lambda e, m=m, half=half, pt=pt: e.tensor_copy(
                out=xn0t[:].rearrange("p (k t) -> p k t", k=ND)[
                    :, half * 8:(half + 1) * 8, m * 128:(m + 1) * 128],
                in_=pt[:].rearrange("p (k t) -> p k t", k=8)),
              waits=[(s_tp, vtp)], inc=s_ev)
        vevt = s_ev.n

        vqmm = proj(m, wqs, 2048, pq, [(s_qe, s_qe.n), (s_ev, vevt)])
        vqe = E("vector", lambda e: e.tensor_tensor(out=qf[:], in0=pq[:], in1=bqr[:],
                                                    op=ALU.add),
                waits=[(s_wu, vqmm), (s_tab, vtab)], inc=s_qe)
        vropeq = rope(qf, rq, NHEAD)
        vevq = transposes(16, rq, qstage, [(s_ln, vropeq)])
        E("sync", lambda e, m=m: e.dma_start(
            out=o_qt[m].transpose([1, 0, 2]),
            in_=qstage[:].rearrange("p (h t) -> p h t", h=NHEAD)),
          waits=[(s_ev, vevq)], inc=s_st, inc_amt=16)

        vkmm = proj(m, wks, 512, pkv, [(s_qe, s_qe.n)])
        vke = E("vector", lambda e: e.tensor_tensor(out=kvf[:], in0=pkv[:], in1=bkr[:],
                                                    op=ALU.add),
                waits=[(s_wu, vkmm)], inc=s_qe)
        vropek = rope(kvf, rq, NKV)
        vevk = transposes(4, rq, kstage, [(s_ln, vropek)])
        E("sync", lambda e, m=m: e.dma_start(
            out=o_kt[m].transpose([1, 0, 2]),
            in_=kstage[:, 0:512].rearrange("p (h t) -> p h t", h=NKV)),
          waits=[(s_ev, vevk)], inc=s_st, inc_amt=16)

        vvmm = proj(m, wvs, 512, pkv, [(s_qe, s_qe.n)])
        vvev = E("vector", lambda e: e.tensor_tensor(out=vstage[:], in0=pkv[:], in1=bvr[:],
                                                     op=ALU.add),
                 waits=[(s_wu, vvmm)], inc=s_qe)
        E("sync", lambda e, m=m: e.dma_start(out=o_v[m], in_=vstage[:]),
          waits=[(s_qe, vvev)], inc=s_st, inc_amt=16)

    E("sync", lambda e: e.dma_start(out=o_xn0t, in_=xn0t[:]),
      waits=[(s_ev, s_ev.n), (s_ln, s_ln.n)], inc=s_st, inc_amt=16)
    E("sync", lambda e: e.nop(), waits=[(s_st, s_st.n)])


def host_inputs_launchA(inputs, core):
    x = np.asarray(inputs["L0_attn_x"][core])[:, ORIG_OF_NEW]
    wq_ = np.asarray(inputs["L0_attn_Wq"])[ORIG_OF_NEW][:, RHO_Q]
    wk_ = np.asarray(inputs["L0_attn_Wk"])[ORIG_OF_NEW][:, RHO_K]
    wv_ = np.asarray(inputs["L0_attn_Wv"])[ORIG_OF_NEW]
    cosr, sinr = _rope_tables()
    return dict(x=np.ascontiguousarray(x), wq=np.ascontiguousarray(wq_),
                wk=np.ascontiguousarray(wk_), wv=np.ascontiguousarray(wv_),
                bq=_rep128(np.asarray(inputs["L0_attn_bq"])[RHO_Q]).astype(BF),
                bk=_rep128(np.asarray(inputs["L0_attn_bk"])[RHO_K]).astype(BF),
                bv=_rep128(np.asarray(inputs["L0_attn_bv"])).astype(BF),
                cosr=cosr, sinr=sinr, ident=np.eye(128, dtype=BF),
                cst=_consts())




def build_launchB(nc, ctx):
    i_qt = nc.dram_tensor("i_qt", [NT, NHEAD, 128, 128], BF16, kind="ExternalInput").ap()
    i_kt = nc.dram_tensor("i_kt", [NT, NKV, 128, 128], BF16, kind="ExternalInput").ap()
    i_v = nc.dram_tensor("i_v", [NT, 128, 512], BF16, kind="ExternalInput").ap()
    wo = nc.dram_tensor("wo", [D, D], F32, kind="ExternalInput").ap()
    bo_fm = nc.dram_tensor("bo_fm", [128, ND], F32, kind="ExternalInput").ap()
    maskc = nc.dram_tensor("maskc", [128, 128], F32, kind="ExternalInput").ap()
    ident = nc.dram_tensor("ident", [128, 128], BF16, kind="ExternalInput").ap()
    o_attnt = nc.dram_tensor("o_attnt", [128, ND * T], BF16, kind="ExternalOutput").ap()
    o_ot = nc.dram_tensor("o_ot", [128, NHEAD * T], BF16, kind="ExternalOutput").ap()

    ar = Arena(nc, 206 * KB)
    off = 0
    ot_all = ar.at("ot_all", off, [128, NHEAD * T], BF16); off += 64 * KB
    wos = ar.at("wos", off, [128, ND * D], BF16); off += 64 * KB
    kt_sb = ar.at("kt_sb", off, [128, NKV * T], BF16); off += 16 * KB
    vsb = ar.at("vsb", off, [128, NT * 512], BF16); off += 16 * KB
    ptb = ar.at("ptb", off, [128, NT * 512], BF16); off += 16 * KB
    qth = [ar.at(f"qth{i}", off + i * 4 * KB, [128, T], BF16) for i in range(2)]; off += 8 * KB
    p_sb = ar.at("p_sb", off, [128, T], BF16); off += 4 * KB
    atst = [ar.at(f"atst{i}", off + i * 4 * KB, [128, T], BF16) for i in range(2)]; off += 8 * KB
    smask = ar.at("smask", off, [128, 128], F32); off += 512
    sid = ar.at("sid", off, [128, 128], BF16); off += 256
    sbo = ar.at("sbo", off, [128, ND], F32); off += 64
    sums = ar.at("sums", off, [128, 8], F32); off += 32
    rvv = ar.at("rvv", off, [128, 4], F32); off += 16
    assert off <= 206 * KB, off

    ps = Psum(nc)
    psc = ps.at("psc", 0, [128, 2048], F32)
    ptp = [ps.at("ptp0", 4, [128, 1024], BF16),
           ps.at("ptp1", 5, [128, 1024], BF16)]
    pot = ps.at("pot", 6, [128, 512], F32)
    pw = [ps.at("pw0", 0, [128, 2048], F32),
          ps.at("pw1", 4, [128, 2048], F32)]

    s_pre = ctx.sem("s_pre"); s_qh = ctx.sem("s_qh"); s_mm = ctx.sem("s_mm")
    s_mx = ctx.sem("s_mx"); s_ex = ctx.sem("s_ex"); s_ps = ctx.sem("s_ps")
    s_tp = ctx.sem("s_tp"); s_ev = ctx.sem("s_ev"); s_st = ctx.sem("s_st")
    s_w = ctx.sem("s_w")
    E = ctx.emit

    E("sync", lambda e: e.dma_start(out=smask[:], in_=maskc), inc=s_pre, inc_amt=16)
    E("sync", lambda e: e.dma_start(out=sid[:], in_=ident), inc=s_pre, inc_amt=16)
    E("sync", lambda e: e.dma_start(out=sbo[:], in_=bo_fm), inc=s_pre, inc_amt=16)
    for g in range(NKV):
        E("sync", lambda e, g=g: e.dma_start(
            out=kt_sb[:, g * T:(g + 1) * T].rearrange("p (m t) -> p m t", m=NT),
            in_=i_kt[:, g].transpose([1, 0, 2])), inc=s_pre, inc_amt=16)
    E("sync", lambda e: e.dma_start(
        out=vsb[:].rearrange("p (m c) -> p m c", m=NT),
        in_=i_v.transpose([1, 0, 2])), inc=s_pre, inc_amt=16)
    for k in range(ND):
        E("gpsimd", lambda e, k=k: e.dma_start(out=wos[:, k * D:(k + 1) * D],
                                               in_=wo[k * 128:(k + 1) * 128, :]),
          inc=s_w, inc_amt=16)
    vpre = s_pre.n
    vwall = s_w.n

    vexp = {}
    for h in range(NHEAD):
        g = h // 4
        hb = qth[h % 2]
        E("sync", lambda e, h=h, hb=hb: e.dma_start(
            out=hb[:].rearrange("p (m t) -> p m t", m=NT),
            in_=i_qt[:, h].transpose([1, 0, 2])),
          waits=[(s_pre, vpre if h == 0 else 0), (s_mm, vexp.get(("smm", h - 2), 0))],
          inc=s_qh, inc_amt=16)
        vqh = s_qh.n

        for c in range(4):
            vptev_chunk = []
            for i in range(4):
                mq = 4 * c + i
                tk_end = 128 * (mq + 1)
                nch = (tk_end + 511) // 512
                for cc in range(nch):
                    w = min(512, tk_end - cc * 512)
                    E("tensor", lambda e, mq=mq, cc=cc, w=w, g=g, hb=hb: e.matmul(
                        out=psc[:, cc * 512: cc * 512 + w],
                        lhsT=hb[:, mq * 128:(mq + 1) * 128],
                        rhs=kt_sb[:, g * T + cc * 512: g * T + cc * 512 + w],
                        start=True, stop=True),
                      waits=([(s_qh, vqh), (s_ex, s_ex.n), (s_ev, s_ev.n)]
                             if cc == 0 else []),
                      inc=(s_mm if cc == nch - 1 else None))
                vS = s_mm.n
                E("vector", lambda e, mq=mq, tk_end=tk_end: e.tensor_tensor(
                    out=psc[:, tk_end - 128:tk_end], in0=psc[:, tk_end - 128:tk_end],
                    in1=smask[:], op=ALU.add),
                  waits=[(s_mm, vS), (s_pre, vpre)])
                vmx = E("vector", lambda e, tk_end=tk_end: e.reduce_max(
                    out=rvv[:, 1:2], in_=psc[:, 0:tk_end], axis=AX.X, negate=True),
                        inc=s_mx)
                vex = E("scalar", lambda e, i=i, tk_end=tk_end: e.activation(
                    out=p_sb[:, 0:tk_end], in_=psc[:, 0:tk_end], func=AF.Exp,
                    bias=rvv[:, 1:2], accum_out=sums[:, i:i + 1]),
                        waits=[(s_mx, vmx)], inc=s_ex)
                vexp[("ex", mq)] = vex
                v1 = E("vector", lambda e, i=i: e.tensor_scalar_mul(
                    out=rvv[:, 2:3], in0=sums[:, i:i + 1], scalar1=float(np.sqrt(2048.0))),
                  waits=[(s_ex, vex)], inc=s_mx)
                v2 = E("vector", lambda e: e.reciprocal(out=rvv[:, 3:4], in_=rvv[:, 2:3]),
                       waits=[(s_mx, v1)], inc=s_mx)
                vps = E("vector", lambda e, tk_end=tk_end: e.tensor_scalar_mul(
                    out=p_sb[:, 0:tk_end], in0=p_sb[:, 0:tk_end], scalar1=rvv[:, 3:4]),
                        waits=[(s_mx, v2)], inc=s_ps)
                ntp = mq + 1
                for half in range((ntp + 7) // 8):
                    pt = ptp[half % 2]
                    cnt = min(8, ntp - half * 8)
                    for kk in range(cnt):
                        j = half * 8 + kk
                        E("tensor", lambda e, j=j, kk=kk, pt=pt: e.transpose(
                            out=pt[:, kk * 128:(kk + 1) * 128],
                            in_=p_sb[:, j * 128:(j + 1) * 128], identity=sid[:]),
                          waits=([(s_ps, vps), (s_ev, s_ev.n)] if kk == 0 else []),
                          inc=(s_tp if kk == cnt - 1 else None))
                    vtp = s_tp.n
                    vpe = E("vector", lambda e, half=half, pt=pt, cnt=cnt, i=i: e.tensor_copy(
                        out=ptb[:].rearrange("p (j q) -> p j q", j=NT)[
                            :, half * 8: half * 8 + cnt, i * 128:(i + 1) * 128],
                        in_=pt[:, 0:cnt * 128].rearrange("p (j q) -> p j q", j=cnt)),
                            waits=[(s_tp, vtp),
                                   (s_mm, vexp.get(("pv", h, c - 1) if c > 0
                                                   else ("pv", h - 1, 3), 0))],
                            inc=s_ev)
                    vptev_chunk.append(vpe)
            njj = 4 * c + 4
            for j in range(njj):
                co = max(0, (j - 4 * c)) * 128
                E("tensor", lambda e, j=j, g=g, njj=njj, co=co: e.matmul(
                    out=pot[:, co:512],
                    lhsT=vsb[:, j * 512 + g * 128: j * 512 + (g + 1) * 128],
                    rhs=ptb[:, j * 512 + co:(j + 1) * 512],
                    start=(j == 0), stop=(j == njj - 1)),
                  waits=([(s_ev, vptev_chunk[-1])] if j == 0 else []),
                  inc=(s_mm if j == njj - 1 else None))
            vpv = s_mm.n
            vexp[("pv", h, c)] = vpv
            E("vector", lambda e, h=h, c=c: e.tensor_copy(
                out=ot_all[:, h * T + c * 512: h * T + (c + 1) * 512], in_=pot[:]),
              waits=[(s_mm, vpv)], inc=s_ev)
        vexp[("smm", h)] = s_mm.n

    vheads = s_ev.n
    E("sync", lambda e: e.dma_start(out=o_ot, in_=ot_all[:]),
      waits=[(s_ev, vheads)], inc=s_st, inc_amt=16)
    for dt in range(ND):
        pwb = pw[1]
        for et in range(ND):
            for tc in range(4):
                E("tensor", lambda e, dt=dt, et=et, tc=tc, pwb=pwb: e.matmul(
                    out=pwb[:, tc * 512:(tc + 1) * 512],
                    lhsT=wos[:, et * D + dt * 128: et * D + dt * 128 + 128],
                    rhs=ot_all[:, et * T + tc * 512: et * T + (tc + 1) * 512],
                    start=(et == 0), stop=(et == ND - 1)),
                  waits=([(s_ev, vheads), (s_w, vwall), (s_st, s_st.n)]
                         if (et == 0 and tc == 0) else []),
                  inc=(s_mm if (et == ND - 1 and tc == 3) else None))
        vwo = s_mm.n
        ab = atst[dt % 2]
        vae = E("vector", lambda e, dt=dt, ab=ab: e.tensor_scalar(
            out=ab[:], in0=pwb[:], scalar1=sbo[:, dt:dt + 1], scalar2=None, op0=ALU.add),
                waits=[(s_mm, vwo), (s_st, max(0, s_st.n - 16))], inc=s_ev)
        E("sync", lambda e, dt=dt, ab=ab: e.dma_start(
            out=o_attnt[:, dt * T:(dt + 1) * T], in_=ab[:]),
          waits=[(s_ev, vae)], inc=s_st, inc_amt=16)
    E("sync", lambda e: e.nop(), waits=[(s_st, s_st.n)])


def host_inputs_launchB(inputs, ra):
    wo_ = np.asarray(inputs["L0_attn_Wo"])[:, ORIG_OF_NEW]
    bo_ = np.asarray(inputs["L0_attn_bo"])[ORIG_OF_NEW]
    mask = np.triu(np.full((128, 128), -1e30, np.float32), 1)
    return dict(i_qt=ra["o_qt"], i_kt=ra["o_kt"], i_v=ra["o_v"],
                wo=np.ascontiguousarray(wo_),
                bo_fm=np.ascontiguousarray(bo_.reshape(ND, 128).T.astype(np.float32)),
                maskc=mask, ident=np.eye(128, dtype=BF))




def build_launchC(nc, ctx):
    i_xn0t = nc.dram_tensor("i_xn0t", [128, ND * T], BF16, kind="ExternalInput").ap()
    gub = nc.dram_tensor("gub", [4, 256, 2 * I_FFN], U8, kind="ExternalInput").ap()
    gus = nc.dram_tensor("gus", [256, 2 * I_FFN], F16, kind="ExternalInput").ap()
    o_h = nc.dram_tensor("o_h", [I_FFN, T], BF16, kind="ExternalOutput").ap()

    ar = Arena(nc, 206 * KB)
    off = 0
    xn0t = ar.at("xn0t", off, [128, ND * T], BF16); off += 64 * KB
    ssc = ar.at("ssc", off, [128, 2 * 2 * I_FFN], BF16); off += 44 * KB
    wd = ar.at("wd", off, [128, ND * 1408], BF16); off += 44 * KB
    bts = [ar.at(f"bts{i}", off + i * 1408, [128, 1408], U8) for i in range(2)]; off += 2816
    dq1 = ar.at("dq1", off, [128, 1408], U8); off += 1408
    dq2 = ar.at("dq2", off, [128, 1408], U8); off += 1408
    dqf = ar.at("dqf", off, [128, 1408], U8); off += 1408
    dqv = ar.at("dqv", off, [128, 1408], BF16); off += 2816
    gs = ar.at("gs", off, [128, 2048], BF16); off += 4 * KB
    hst = [ar.at(f"hst{i}", off + i * 4 * KB, [128, 2048], BF16) for i in range(2)]; off += 8 * KB
    assert off <= 206 * KB, off

    ps = Psum(nc)
    pg = ps.at("pg", 0, [128, 2048], F32)
    pu = ps.at("pu", 4, [128, 2048], F32)

    s_pre = ctx.sem("s_pre"); s_b = ctx.sem("s_b"); s_dq = ctx.sem("s_dq")
    s_mm = ctx.sem("s_mm"); s_ev = ctx.sem("s_ev"); s_st = ctx.sem("s_st")
    E = ctx.emit

    E("sync", lambda e: e.dma_start(out=xn0t[:], in_=i_xn0t), inc=s_pre, inc_amt=16)
    for hf in range(2):
        E("gpsimd", lambda e, hf=hf: e.dma_start(
            out=ssc[:, hf * 2 * I_FFN:(hf + 1) * 2 * I_FFN],
            in_=gus[hf * 128:(hf + 1) * 128, :]), inc=s_pre, inc_amt=16)
    vpre = s_pre.n

    NOC = 8
    for oc in range(NOC):
        vdq_all = None
        for k in range(ND):
            p = k // 4
            hf = k % 2
            bt = bts[k % 2]
            vbt = E("sync", lambda e, p=p, hf=hf, oc=oc, bt=bt: e.dma_start(
                out=bt[:].rearrange("q (s c) -> q s c", s=2),
                in_=gub[p, hf * 128:(hf + 1) * 128].rearrange(
                    "q (s i) -> q s i", s=2)[:, :, oc * 704:(oc + 1) * 704]),
                waits=[(s_dq, max(0, s_dq.n - 4))], inc=s_b, inc_amt=16)
            def scale_ap(hf=hf, oc=oc):
                base = ssc[:, hf * 2 * I_FFN:(hf + 1) * 2 * I_FFN]
                return base.rearrange("q (s i) -> q s i", s=2)[:, :, oc * 704:(oc + 1) * 704]
            j = k // 2
            lo = (j % 2 == 0)
            if lo:
                E("vector", lambda e, bt=bt: e.tensor_scalar(
                    out=dq1[:], in0=bt[:], scalar1=7, scalar2=2,
                    op0=ALU.bitwise_and, op1=ALU.logical_shift_left),
                  waits=[(s_b, vbt), (s_mm, s_mm.n)], inc=s_dq)
                E("vector", lambda e, bt=bt: e.tensor_scalar(
                    out=dq2[:], in0=bt[:], scalar1=8, scalar2=4,
                    op0=ALU.bitwise_and, op1=ALU.logical_shift_left), inc=s_dq)
            else:
                E("vector", lambda e, bt=bt: e.tensor_scalar(
                    out=dq1[:], in0=bt[:], scalar1=0x70, scalar2=2,
                    op0=ALU.bitwise_and, op1=ALU.logical_shift_right),
                  waits=[(s_b, vbt), (s_mm, s_mm.n)], inc=s_dq)
                E("vector", lambda e, bt=bt: e.tensor_scalar(
                    out=dq2[:], in0=bt[:], scalar1=0x80, scalar2=None,
                    op0=ALU.bitwise_and), inc=s_dq)
            vb = E("vector", lambda e: e.tensor_tensor(out=dqf[:], in0=dq1[:], in1=dq2[:],
                                                       op=ALU.bitwise_or),
                   waits=[(s_dq, s_dq.n)], inc=s_dq)
            vc = E("vector", lambda e: e.tensor_copy(out=dqv[:],
                                                     in_=dqf[:].bitcast(mybir.dt.float8e4)),
                   waits=[(s_dq, vb)], inc=s_dq)
            vdq_all = E("vector", lambda e, k=k, scale_ap=scale_ap: e.tensor_tensor(
                out=wd[:, k * 1408:(k + 1) * 1408].rearrange("q (s c) -> q s c", s=2),
                in0=dqv[:].rearrange("q (s c) -> q s c", s=2), in1=scale_ap(),
                op=ALU.mult), waits=[(s_dq, vc)], inc=s_dq)

        blocks = [(b * 128, min((b + 1) * 128, 704)) for b in range(6)]
        for b, (c0, c1) in enumerate(blocks):
            w = c1 - c0
            for side, pp in ((0, pg), (1, pu)):
                for k in range(ND):
                    for tc in range(4):
                        E("tensor", lambda e, k=k, tc=tc, side=side, c0=c0, w=w, pp=pp:
                          e.matmul(out=pp[0:w, tc * 512:(tc + 1) * 512],
                                   lhsT=wd[:, k * 1408 + side * 704 + c0:
                                           k * 1408 + side * 704 + c0 + w],
                                   rhs=xn0t[:, k * T + tc * 512: k * T + (tc + 1) * 512],
                                   start=(k == 0), stop=(k == ND - 1)),
                          waits=([(s_dq, vdq_all), (s_pre, vpre),
                                  (s_ev, s_ev.n)] if (k == 0 and tc == 0) else []),
                          inc=(s_mm if (k == ND - 1 and tc == 3) else None))
            vg = s_mm.n
            hb = hst[b % 2]
            E("scalar", lambda e, w=w: e.activation(out=gs[0:w, :], in_=pg[0:w, :],
                                                    func=AF.Silu, scale=64.0),
              waits=[(s_mm, vg)], inc=s_ev)
            vh = E("vector", lambda e, w=w, hb=hb: e.scalar_tensor_tensor(
                out=hb[0:w, :], in0=pu[0:w, :], scalar=64.0, in1=gs[0:w, :],
                op0=ALU.mult, op1=ALU.mult),
                   waits=[(s_ev, s_ev.n), (s_st, max(0, s_st.n - 16))], inc=s_ev)
            E("sync", lambda e, oc=oc, c0=c0, c1=c1, hb=hb, w=w: e.dma_start(
                out=o_h[oc * 704 + c0: oc * 704 + c1, :], in_=hb[0:w, :]),
              waits=[(s_ev, vh)], inc=s_st, inc_amt=16)
    E("sync", lambda e: e.nop(), waits=[(s_st, s_st.n)])


def host_inputs_launchC(inputs, ra):
    gup = np.asarray(inputs["L0_ffn_gate_up_packed"])
    rows = np.concatenate([I_ORIG_OF_NEW, I_FFN + I_ORIG_OF_NEW])
    gup = gup[rows]
    byts = gup.view(np.uint8).reshape(2 * I_FFN, 256, 4)
    gub = np.ascontiguousarray(byts.transpose(2, 1, 0))
    sc = np.asarray(inputs["L0_ffn_gate_up_scales"])
    sc = sc[:, rows]
    gus = np.ascontiguousarray(np.repeat(sc, 16, axis=0))
    return dict(i_xn0t=ra["o_xn0t"], gub=gub, gus=gus)



def build_launchD(nc, ctx):
    dnb = nc.dram_tensor("dnb", [4, 704, D], U8, kind="ExternalInput").ap()
    dns = nc.dram_tensor("dns", [704, D], F16, kind="ExternalInput").ap()
    i_h = nc.dram_tensor("i_h", [I_FFN, T], BF16, kind="ExternalInput").ap()
    i_attnt = nc.dram_tensor("i_attnt", [128, ND * T], BF16, kind="ExternalInput").ap()
    x = nc.dram_tensor("x", [T, D], F32, kind="ExternalInput").ap()
    ident = nc.dram_tensor("ident", [128, 128], BF16, kind="ExternalInput").ap()
    cst = nc.dram_tensor("cst", [128, 8], F32, kind="ExternalInput").ap()
    onesc = nc.dram_tensor("onesc", [128, 1], BF16, kind="ExternalInput").ap()
    onesr = nc.dram_tensor("onesr", [1, 128], BF16, kind="ExternalInput").ap()
    o_wdn = nc.dram_tensor("o_wdn", [44, 128, D], BF16, kind="Internal").ap()
    o_x1 = nc.dram_tensor("o_x1", [T, D], F32, kind="ExternalOutput").ap()
    o_xn1t = nc.dram_tensor("o_xn1t", [128, ND * T], BF16, kind="ExternalOutput").ap()

    ar = Arena(nc, 206 * KB)
    off = 0
    xn1t = ar.at("xn1t", off, [128, ND * T], BF16); off += 64 * KB
    yts = ar.at("yts", off, [128, ND * 512], BF16); off += 16 * KB
    y2 = ar.at("y2", off, [128, 512], BF16); off += 1 * KB
    wst = [ar.at(f"wst{i}", off + i * 2 * KB, [128, 1024], BF16) for i in range(2)]; off += 4 * KB
    hss = [ar.at(f"hss{i}", off + i * 1 * KB, [128, 512], BF16) for i in range(2)]; off += 2 * KB
    ats = ar.at("ats", off, [128, 512], BF16); off += 1 * KB
    acct = ar.at("acct", off, [128, ND * 512], BF16); off += 16 * KB
    xtm = ar.at("xtm", off, [128, D], F32); off += 8 * KB
    x1s = ar.at("x1s", off, [128, D], F32); off += 8 * KB
    atok = ar.at("atok", off, [128, D], BF16); off += 4 * KB
    xn1s = ar.at("xn1s", off, [128, D], BF16); off += 4 * KB
    sq = ar.at("sq", off, [128, D], F32); off += 8 * KB
    rmsv = ar.at("rmsv", off, [1, 512], F32); off += 2 * KB
    rmsp = [ar.at(f"rmsp{i}", off + i * 2 * KB, [1, 512], F32) for i in range(2)]; off += 4 * KB
    rmsb = ar.at("rmsb", off, [1, 512], BF16); off += 1 * KB
    sid = ar.at("sid", off, [128, 128], BF16); off += 256
    sonesc = ar.at("sonesc", off, [128, 1], BF16); off += 32
    sonesr = ar.at("sonesr", off, [1, 128], BF16); off += 256
    scst = ar.at("scst", off, [128, 8], F32); off += 32
    stat = ar.at("stat", off, [128, 8], F32); off += 32
    btd = [ar.at(f"btd{i}", off + i * 2 * KB, [128, D], U8) for i in range(2)]; off += 4 * KB
    dq1 = ar.at("dq1", off, [128, D], U8); off += 2 * KB
    dq2 = ar.at("dq2", off, [128, D], U8); off += 2 * KB
    dqf = ar.at("dqf", off, [128, D], U8); off += 2 * KB
    dqv = ar.at("dqv", off, [128, D], BF16); off += 4 * KB
    dsc = [ar.at(f"dsc{i}", off + i * 4 * KB, [128, D], BF16) for i in range(2)]; off += 8 * KB
    wdo = [ar.at(f"wdo{i}", off + i * 4 * KB, [128, D], BF16) for i in range(2)]; off += 8 * KB
    assert off <= 206 * KB, off

    ps = Psum(nc)
    pd = [ps.at(f"pd{i}", i, [128, 512], F32) for i in range(8)]
    prs = ps.at("prs", 0, [1, 512], F32)
    prb = ps.at("prb", 1, [128, 512], F32)
    ptr = ps.at("ptr", 2, [128, 1024], BF16)
    ptr2 = ps.at("ptr2", 3, [128, 1024], BF16)

    s_pre = ctx.sem("s_pre"); s_b = ctx.sem("s_b"); s_dq = ctx.sem("s_dq")
    s_w = ctx.sem("s_w"); s_h = ctx.sem("s_h"); s_mm = ctx.sem("s_mm")
    s_ev = ctx.sem("s_ev"); s_st = ctx.sem("s_st"); s_tp = ctx.sem("s_tp")
    s_ln = ctx.sem("s_ln"); s_x = ctx.sem("s_x"); s_at = ctx.sem("s_at")
    E = ctx.emit

    for src_, dst in ((ident, sid), (cst, scst), (onesc, sonesc), (onesr, sonesr)):
        E("sync", lambda e, src_=src_, dst=dst: e.dma_start(out=dst[:], in_=src_),
          inc=s_pre, inc_amt=16)
    vpre = s_pre.n

    for it in range(44):
        r0 = it * 128
        j0 = r0 // 704
        ip0 = r0 - j0 * 704
        pieces = []
        left = 128
        rr = 0
        while left > 0:
            take = min(left, 704 - ip0)
            pieces.append((j0, ip0, rr, take))
            rr += take; left -= take
            j0 += 1; ip0 = 0
        bt = btd[it % 2]
        sc_t = dsc[it % 2]
        vbts = []
        for (j, ip, rr, take) in pieces:
            p = j // 2
            E("sync", lambda e, p=p, ip=ip, rr=rr, take=take, bt=bt: e.dma_start(
                out=bt[rr:rr + take, :], in_=dnb[p, ip:ip + take, :]),
              waits=[(s_dq, s_dq.n)], inc=s_b, inc_amt=16)
            E("gpsimd", lambda e, ip=ip, rr=rr, take=take, sc_t=sc_t: e.dma_start(
                out=sc_t[rr:rr + take, :], in_=dns[ip:ip + take, :]),
              waits=[(s_dq, s_dq.n)], inc=s_b, inc_amt=16)
        vbt = s_b.n
        for (j, ip, rr, take) in pieces:
            lo = (j % 2 == 0)
            if lo:
                E("vector", lambda e, rr=rr, take=take, bt=bt: e.tensor_scalar(
                    out=dq1[rr:rr + take, :], in0=bt[rr:rr + take, :], scalar1=7,
                    scalar2=2, op0=ALU.bitwise_and, op1=ALU.logical_shift_left),
                  waits=[(s_b, vbt)], inc=s_dq)
                E("vector", lambda e, rr=rr, take=take, bt=bt: e.tensor_scalar(
                    out=dq2[rr:rr + take, :], in0=bt[rr:rr + take, :], scalar1=8,
                    scalar2=4, op0=ALU.bitwise_and, op1=ALU.logical_shift_left), inc=s_dq)
            else:
                E("vector", lambda e, rr=rr, take=take, bt=bt: e.tensor_scalar(
                    out=dq1[rr:rr + take, :], in0=bt[rr:rr + take, :], scalar1=0x70,
                    scalar2=2, op0=ALU.bitwise_and, op1=ALU.logical_shift_right),
                  waits=[(s_b, vbt)], inc=s_dq)
                E("vector", lambda e, rr=rr, take=take, bt=bt: e.tensor_scalar(
                    out=dq2[rr:rr + take, :], in0=bt[rr:rr + take, :], scalar1=0x80,
                    scalar2=None, op0=ALU.bitwise_and), inc=s_dq)
        vb = E("vector", lambda e: e.tensor_tensor(out=dqf[:], in0=dq1[:], in1=dq2[:],
                                                   op=ALU.bitwise_or),
               waits=[(s_dq, s_dq.n)], inc=s_dq)
        vc = E("vector", lambda e: e.tensor_copy(out=dqv[:],
                                                 in_=dqf[:].bitcast(mybir.dt.float8e4)),
               waits=[(s_dq, vb)], inc=s_dq)
        wo_t = wdo[it % 2]
        vdq = E("vector", lambda e, wo_t=wo_t, sc_t=sc_t: e.tensor_tensor(
            out=wo_t[:], in0=dqv[:], in1=sc_t[:], op=ALU.mult),
                waits=[(s_st, max(0, s_st.n - 16)), (s_dq, vc)], inc=s_dq)
        E("sync", lambda e, it=it, wo_t=wo_t: e.dma_start(out=o_wdn[it], in_=wo_t[:]),
          waits=[(s_dq, vdq)], inc=s_st, inc_amt=16)

    vph1 = s_st.n
    for tc in range(4):
        for dh in range(2):
            base_mm = s_mm.n
            for i in range(44):
                wb = wst[i % 2]
                hb = hss[i % 2]
                wwait = [(s_mm, base_mm + i - 1 if i >= 2 else base_mm)]
                E("gpsimd", lambda e, i=i, dh=dh, wb=wb: e.dma_start(
                    out=wb[:], in_=o_wdn[i, :, dh * 1024:(dh + 1) * 1024]),
                  waits=wwait + ([(s_st, vph1)] if (tc == 0 and dh == 0 and i < 2) else []),
                  inc=s_w, inc_amt=16)
                E("sync", lambda e, i=i, tc=tc, hb=hb: e.dma_start(
                    out=hb[:], in_=i_h[i * 128:(i + 1) * 128, tc * 512:(tc + 1) * 512]),
                  waits=wwait, inc=s_h, inc_amt=16)
                vw = s_w.n; vh = s_h.n
                for d8 in range(8):
                    E("tensor", lambda e, i=i, d8=d8, wb=wb, hb=hb: e.matmul(
                        out=pd[d8][:], lhsT=wb[:, d8 * 128:(d8 + 1) * 128], rhs=hb[:],
                        start=(i == 0), stop=(i == 43)),
                      waits=([(s_w, vw), (s_h, vh)] + ([(s_ev, s_ev.n)] if i == 0 else []))
                      if d8 == 0 else [],
                      inc=(s_mm if d8 == 7 else None))
            vdmm = s_mm.n
            for d8 in range(8):
                dt = dh * 8 + d8
                vy = E("scalar", lambda e, d8=d8, dt=dt: e.activation(
                    out=yts[:, dt * 512:(dt + 1) * 512], in_=pd[d8][:], func=AF.Copy,
                    scale=64.0), waits=[(s_mm, vdmm)], inc=s_ev)
                vy2 = E("scalar", lambda e, d8=d8: e.activation(
                    out=y2[:], in_=pd[d8][:], func=AF.Square, scale=64.0),
                        waits=[(s_mm, s_mm.n)], inc=s_ev)
                E("tensor", lambda e, d8=d8: e.matmul(
                    out=prs[:], lhsT=sonesc[:], rhs=y2[:],
                    start=(d8 == 0), stop=(d8 == 7)),
                  waits=[(s_ev, vy2)], inc=s_mm)
            vos = s_mm.n
            vpr = E("vector", lambda e, dh=dh: e.tensor_copy(
                out=rmsp[dh][:], in_=prs[:]), waits=[(s_mm, vos)], inc=s_ev)
            if dh == 0:
                vpr0 = vpr
        vrs = s_mm.n
        v0 = E("vector", lambda e: e.tensor_tensor(out=rmsp[0][:], in0=rmsp[0][:],
                                                   in1=rmsp[1][:], op=ALU.add),
               waits=[(s_ev, vpr)], inc=s_ev)
        v1 = E("scalar", lambda e: e.activation(out=rmsv[:], in_=rmsp[0][:], func=AF.Sqrt,
                                                scale=1.0 / D, bias=scst[0:1, 1:2]),
               waits=[(s_ev, v0)], inc=s_ev)
        v2 = E("vector", lambda e: e.reciprocal(out=rmsv[:], in_=rmsv[:]),
               waits=[(s_ev, v1)], inc=s_ev)
        v3 = E("vector", lambda e: e.tensor_copy(out=rmsb[:], in_=rmsv[:]),
               waits=[(s_ev, v2)], inc=s_ev)
        vbc = E("tensor", lambda e: e.matmul(out=prb[:], lhsT=sonesr[:], rhs=rmsb[:],
                                             start=True, stop=True),
                waits=[(s_ev, v3), (s_pre, vpre)], inc=s_mm)
        for dt in range(ND):
            va = E("sync", lambda e, dt=dt, tc=tc: e.dma_start(
                out=ats[:], in_=i_attnt[:, dt * T + tc * 512: dt * T + (tc + 1) * 512]),
                   waits=[(s_at, s_at.n)], inc=s_h, inc_amt=16)
            vm = E("vector", lambda e, dt=dt: e.tensor_tensor(
                out=y2[:], in0=yts[:, dt * 512:(dt + 1) * 512], in1=prb[:], op=ALU.mult),
                   waits=[(s_mm, vbc), (s_h, va)], inc=s_ev)
            E("vector", lambda e, dt=dt: e.tensor_tensor(
                out=acct[:, dt * 512:(dt + 1) * 512], in0=y2[:], in1=ats[:], op=ALU.add),
              inc=s_at)
        vacc = s_at.n
        for mi in range(4):
            m = tc * 4 + mi
            E("sync", lambda e, m=m: e.dma_start(out=xtm[:], in_=x[m * 128:(m + 1) * 128, :]),
              waits=[(s_ln, s_ln.n)], inc=s_x, inc_amt=16)
            vx = s_x.n
            for half in range(2):
                pt = ptr if half == 0 else ptr2
                for kk in range(8):
                    dt = half * 8 + kk
                    E("tensor", lambda e, dt=dt, kk=kk, mi=mi, pt=pt: e.transpose(
                        out=pt[:, kk * 128:(kk + 1) * 128],
                        in_=acct[:, dt * 512 + mi * 128: dt * 512 + (mi + 1) * 128],
                        identity=sid[:]),
                      waits=([(s_at, vacc), (s_ev, s_ev.n)] if kk == 0 else []),
                      inc=(s_tp if kk == 7 else None))
                vtp = s_tp.n
                E("vector", lambda e, half=half, pt=pt: e.tensor_copy(
                    out=atok[:, half * 1024:(half + 1) * 1024], in_=pt[:]),
                  waits=[(s_tp, vtp)], inc=s_ev)
            vat = s_ev.n
            vx1 = E("vector", lambda e: e.tensor_tensor(out=x1s[:], in0=xtm[:],
                                                        in1=atok[:], op=ALU.add),
                    waits=[(s_ev, vat), (s_x, vx)], inc=s_ev)
            E("sync", lambda e, m=m: e.dma_start(out=o_x1[m * 128:(m + 1) * 128, :],
                                                 in_=x1s[:]),
              waits=[(s_ev, vx1)], inc=s_st, inc_amt=16)
            E("scalar", lambda e: e.activation(out=sq[:], in_=x1s[:], func=AF.Copy,
                                               accum_out=stat[:, 2:3]),
              waits=[(s_ev, vx1)])
            E("scalar", lambda e: e.activation(out=sq[:], in_=x1s[:], func=AF.Square,
                                               accum_out=stat[:, 3:4]))
            E("scalar", lambda e: e.activation(out=stat[:, 0:1], in_=stat[:, 2:3],
                                               func=AF.Copy, scale=1.0 / D))
            E("scalar", lambda e: e.activation(out=stat[:, 4:5], in_=stat[:, 3:4],
                                               func=AF.Copy, scale=1.0 / D))
            vs5 = E("scalar", lambda e: e.activation(out=stat[:, 5:6], in_=stat[:, 0:1],
                                                     func=AF.Square), inc=s_ev)
            vv6 = E("vector", lambda e: e.tensor_tensor(out=stat[:, 6:7], in0=stat[:, 4:5],
                                                        in1=stat[:, 5:6], op=ALU.subtract),
                    waits=[(s_ev, vs5)], inc=s_ev)
            vs7 = E("scalar", lambda e: e.activation(out=stat[:, 7:8], in_=stat[:, 6:7],
                                                     func=AF.Sqrt, bias=scst[:, 0:1]),
                    waits=[(s_ev, vv6)], inc=s_ev)
            vr = E("vector", lambda e: e.reciprocal(out=stat[:, 1:2], in_=stat[:, 7:8]),
                   waits=[(s_ev, vs7)], inc=s_ev)
            vln = E("vector", lambda e: e.tensor_scalar(
                out=xn1s[:], in0=x1s[:], scalar1=stat[:, 0:1], scalar2=stat[:, 1:2],
                op0=ALU.subtract, op1=ALU.mult), waits=[(s_ev, vr)], inc=s_ln)
            for half in range(2):
                pt = ptr if half == 0 else ptr2
                for kk in range(8):
                    k = half * 8 + kk
                    E("tensor", lambda e, k=k, kk=kk, pt=pt: e.transpose(
                        out=pt[:, kk * 128:(kk + 1) * 128],
                        in_=xn1s[:, k * 128:(k + 1) * 128], identity=sid[:]),
                      waits=([(s_ln, vln), (s_ev, s_ev.n)] if kk == 0 else []),
                      inc=(s_tp if kk == 7 else None))
                vtp = s_tp.n
                E("vector", lambda e, m=m, half=half, pt=pt: e.tensor_copy(
                    out=xn1t[:].rearrange("p (k t) -> p k t", k=ND)[
                        :, half * 8:(half + 1) * 8, m * 128:(m + 1) * 128],
                    in_=pt[:].rearrange("p (k t) -> p k t", k=8)),
                  waits=[(s_tp, vtp)], inc=s_ev)

    E("sync", lambda e: e.dma_start(out=o_xn1t, in_=xn1t[:]),
      waits=[(s_ev, s_ev.n), (s_ln, s_ln.n)], inc=s_st, inc_amt=16)
    E("sync", lambda e: e.nop(), waits=[(s_st, s_st.n)])


def host_inputs_launchD(inputs, core, rb, rc):
    dnp = np.asarray(inputs["L0_ffn_down_packed"])
    dnp = dnp[ORIG_OF_NEW]
    dnpT = np.ascontiguousarray(dnp.T)
    byts = dnpT.view(np.uint8).reshape(704, D, 4)
    dnb = np.ascontiguousarray(byts.transpose(2, 0, 1))
    sc = np.asarray(inputs["L0_ffn_down_scales"])
    sc = sc[:, ORIG_OF_NEW]
    dns = np.ascontiguousarray(np.repeat(sc, 16, axis=0))
    x = np.asarray(inputs["L0_attn_x"][core])[:, ORIG_OF_NEW]
    return dict(dnb=dnb, dns=dns, i_h=rc["o_h"], i_attnt=rb["o_attnt"],
                x=np.ascontiguousarray(x), ident=np.eye(128, dtype=BF),
                cst=_consts(), onesc=np.ones((128, 1), BF), onesr=np.ones((1, 128), BF))




def build_launchE(nc, ctx):
    i_xn1t = nc.dram_tensor("i_xn1t", [128, ND * T], BF16, kind="ExternalInput").ap()
    gwt = nc.dram_tensor("gwt", [128, 2 * ND], F32, kind="ExternalInput").ap()
    wg = nc.dram_tensor("wg", [2, 64, 128, ND, 128], F32, kind="ExternalInput").ap()
    wu = nc.dram_tensor("wu", [2, 64, 128, ND, 128], F32, kind="ExternalInput").ap()
    wdn = nc.dram_tensor("wdn", [2, ND, 128, 64, 128], F32, kind="ExternalInput").ap()
    onesr = nc.dram_tensor("onesr", [1, 128], BF16, kind="ExternalInput").ap()
    ident = nc.dram_tensor("ident", [128, 128], BF16, kind="ExternalInput").ap()
    o_moet = nc.dram_tensor("o_moet", [128, ND * T], BF16, kind="ExternalOutput").ap()
    h_scr = nc.dram_tensor("h_scr", [2, 64, 128, T], BF16, kind="Internal").ap()

    KBo = KB
    ar = Arena(nc, 206 * KBo)
    xn1t = ar.at("xn1t", 0, [128, ND * T], BF16)
    hch = ar.at("hch", 0, [128, 64 * 512], BF16)
    wgs = [ar.at(f"wgs{i}", 64 * KBo + i * 4 * KBo, [128, ND * 128], BF16) for i in range(2)]
    wus = [ar.at(f"wus{i}", 72 * KBo + i * 4 * KBo, [128, ND * 128], BF16) for i in range(2)]
    wdst = [ar.at("wdst0", 64 * KBo, [128, 64 * 128], BF16),
            ar.at("wdst1", 96 * KBo, [128, 64 * 128], BF16)]
    hstg = [ar.at(f"hstg{i}", 80 * KBo + i * 4 * KBo, [128, T], BF16) for i in range(2)]
    gs = ar.at("gs", 88 * KBo, [128, T], BF16)
    moetc = ar.at("moetc", 80 * KBo, [128, ND * 512], BF16)
    sdif = ar.at("sdif", 96 * KBo, [128, ND], F32)
    stw = ar.at("stw", 96 * KBo + 64, [128, ND], F32)
    smk = ar.at("smk", 96 * KBo + 128, [128, ND], F32)
    srowb = ar.at("srowb", 96 * KBo + 192, [128, 2 * ND], BF16)
    srowt = ar.at("srowt", 96 * KBo + 512, [128, 256], BF16)
    srow2 = ar.at("srow2", 97 * KBo, [1, 2 * T], BF16)
    sid = ar.at("sid", 105 * KBo, [128, 128], BF16)
    w0b = ar.at("w0b", 128 * KBo, [128, T], BF16)
    w1b = ar.at("w1b", 132 * KBo, [128, T], BF16)
    mstg = ar.at("mstg", 136 * KBo, [128, 512], F32)
    sgw = ar.at("sgw", 138 * KBo, [128, 2 * ND], F32)
    sgd = ar.at("sgd", 138 * KBo + 128, [128, ND], BF16)
    sonesr = ar.at("sonesr", 138 * KBo + 192, [1, 128], BF16)

    ps = Psum(nc)
    pg = ps.at("pg", 0, [128, 2048], F32)
    pu = ps.at("pu", 4, [128, 2048], F32)
    pdn = [ps.at(f"pdn{i}", 4 + i, [128, 512], F32) for i in range(2)]
    pgt = ps.at("pgt", 6, [128, 1], F32)
    pbt = ps.at("pbt", 7, [128, 256], BF16)
    pbc = ps.at("pbc", 7, [128, 512], F32)

    s_pre = ctx.sem("s_pre"); s_w = ctx.sem("s_w"); s_h = ctx.sem("s_h")
    s_mm = ctx.sem("s_mm"); s_ev = ctx.sem("s_ev"); s_st = ctx.sem("s_st")
    s_g = ctx.sem("s_g")
    E = ctx.emit

    E("sync", lambda e: e.dma_start(out=xn1t[:], in_=i_xn1t), inc=s_pre, inc_amt=16)
    E("sync", lambda e: e.dma_start(out=sonesr[:], in_=onesr), inc=s_pre, inc_amt=16)
    E("sync", lambda e: e.dma_start(out=sgw[:], in_=gwt), inc=s_pre, inc_amt=16)
    E("sync", lambda e: e.dma_start(out=sid[:], in_=ident), inc=s_pre, inc_amt=16)
    vpre = s_pre.n

    vgd = E("vector", lambda e: e.tensor_tensor(
        out=sgd[:],
        in0=sgw[:].rearrange("p (k two) -> p k two", k=ND)[:, :, 1],
        in1=sgw[:].rearrange("p (k two) -> p k two", k=ND)[:, :, 0],
        op=ALU.subtract), waits=[(s_pre, vpre)], inc=s_g)
    for m in range(NT):
        for k in range(ND):
            E("tensor", lambda e, m=m, k=k: e.matmul(
                out=pgt[:], lhsT=xn1t[:, k * T + m * 128:k * T + (m + 1) * 128],
                rhs=sgd[:, k:k + 1], start=(k == 0), stop=(k == ND - 1)),
              waits=([(s_g, vgd)] + ([(s_g, s_g.n)] if m > 0 else [])) if k == 0 else [],
              inc=(s_mm if k == ND - 1 else None))
        vdc = s_mm.n
        E("vector", lambda e, m=m: e.tensor_copy(out=sdif[:, m:m + 1], in_=pgt[:]),
          waits=[(s_mm, vdc)], inc=s_g)
    vdif = s_g.n
    v1 = E("scalar", lambda e: e.activation(out=stw[:], in_=sdif[:], func=AF.Abs),
           waits=[(s_g, vdif)], inc=s_g)
    v2 = E("scalar", lambda e: e.activation(out=smk[:], in_=stw[:], func=AF.Sigmoid),
           waits=[(s_g, v1)], inc=s_g)
    v3 = E("vector", lambda e: e.tensor_scalar(out=stw[:], in0=sdif[:], scalar1=0.0,
                                               scalar2=None, op0=ALU.is_gt),
           waits=[(s_g, v2)], inc=s_g)
    v4 = E("vector", lambda e: e.tensor_tensor(out=stw[:], in0=stw[:], in1=smk[:],
                                               op=ALU.mult), waits=[(s_g, v3)], inc=s_g)
    v5 = E("vector", lambda e: e.tensor_tensor(out=smk[:], in0=smk[:], in1=stw[:],
                                               op=ALU.subtract), waits=[(s_g, v4)], inc=s_g)
    v6 = E("vector", lambda e: e.tensor_copy(out=srowb[:, 0:ND], in_=smk[:]),
           waits=[(s_g, v5)], inc=s_g)
    v7 = E("vector", lambda e: e.tensor_copy(out=srowb[:, ND:2 * ND], in_=stw[:]),
           waits=[(s_g, v6)], inc=s_g)
    vt0 = E("tensor", lambda e: e.transpose(out=pbt[0:ND, 0:128], in_=srowb[:, 0:ND],
                                            identity=sid[:]),
            waits=[(s_g, v7)], inc=s_mm)
    vt1 = E("tensor", lambda e: e.transpose(out=pbt[0:ND, 128:256], in_=srowb[:, ND:2 * ND],
                                            identity=sid[:]), inc=s_mm)
    vte = E("vector", lambda e: e.tensor_copy(out=srowt[0:ND, :], in_=pbt[0:ND, 0:256]),
            waits=[(s_mm, s_mm.n)], inc=s_g)
    E("gpsimd", lambda e: e.dma_start(out=srow2[:1, 0:T], in_=srowt[0:ND, 0:128]),
      waits=[(s_g, vte)], inc=s_h, inc_amt=16)
    E("gpsimd", lambda e: e.dma_start(out=srow2[:1, T:2 * T], in_=srowt[0:ND, 128:256]),
      inc=s_h, inc_amt=16)
    vfl = s_h.n
    for c in range(4):
        vb0 = E("tensor", lambda e, c=c: e.matmul(
            out=pbc[:], lhsT=sonesr[:], rhs=srow2[:, c * 512:(c + 1) * 512],
            start=True, stop=True), waits=[(s_h, vfl), (s_ev, s_ev.n)], inc=s_mm)
        E("vector", lambda e, c=c: e.tensor_copy(out=w0b[:, c * 512:(c + 1) * 512],
                                                 in_=pbc[:]),
          waits=[(s_mm, vb0)], inc=s_ev)
        vb1 = E("tensor", lambda e, c=c: e.matmul(
            out=pbc[:], lhsT=sonesr[:], rhs=srow2[:, T + c * 512: T + (c + 1) * 512],
            start=True, stop=True), waits=[(s_ev, s_ev.n)], inc=s_mm)
        E("vector", lambda e, c=c: e.tensor_copy(out=w1b[:, c * 512:(c + 1) * 512],
                                                 in_=pbc[:]),
          waits=[(s_mm, vb1)], inc=s_ev)
    vgate = s_ev.n

    for e_i in range(2):
        base_mm = s_mm.n
        for f in range(64):
            wgb = wgs[f % 2]; wub = wus[f % 2]
            wwait = [(s_mm, base_mm + 2 * (f - 1) if f >= 2 else base_mm)]
            E("gpsimd", lambda e, e_i=e_i, f=f, wgb=wgb: e.dma_start(
                out=wgb[:].rearrange("p (k c) -> p k c", k=ND), in_=wg[e_i, f]),
              waits=wwait, inc=s_w, inc_amt=16)
            E("gpsimd", lambda e, e_i=e_i, f=f, wub=wub: e.dma_start(
                out=wub[:].rearrange("p (k c) -> p k c", k=ND), in_=wu[e_i, f]),
              waits=wwait, inc=s_w, inc_amt=16)
            vw = s_w.n
            hb = hstg[f % 2]
            for k in range(ND):
                for c in range(4):
                    E("tensor", lambda e, k=k, c=c, wgb=wgb: e.matmul(
                        out=pg[:, c * 512:(c + 1) * 512],
                        lhsT=wgb[:, k * 128:(k + 1) * 128],
                        rhs=xn1t[:, k * T + c * 512: k * T + (c + 1) * 512],
                        start=(k == 0), stop=(k == ND - 1)),
                      waits=([(s_w, vw), (s_ev, s_ev.n)] if (k == 0 and c == 0) else []),
                      inc=(s_mm if (k == ND - 1 and c == 3) else None))
            for k in range(ND):
                for c in range(4):
                    E("tensor", lambda e, k=k, c=c, wub=wub: e.matmul(
                        out=pu[:, c * 512:(c + 1) * 512],
                        lhsT=wub[:, k * 128:(k + 1) * 128],
                        rhs=xn1t[:, k * T + c * 512: k * T + (c + 1) * 512],
                        start=(k == 0), stop=(k == ND - 1)),
                      waits=([(s_ev, s_ev.n)] if (k == 0 and c == 0) else []),
                      inc=(s_mm if (k == ND - 1 and c == 3) else None))
            vum = s_mm.n
            vsg = E("scalar", lambda e: e.activation(out=gs[:], in_=pg[:], func=AF.Silu),
                    waits=[(s_mm, vum), (s_ev, s_ev.n)], inc=s_ev)
            vh = E("vector", lambda e, hb=hb: e.scalar_tensor_tensor(
                out=hb[:], in0=pu[:], scalar=1.0, in1=gs[:], op0=ALU.mult, op1=ALU.mult),
                   waits=[(s_ev, vsg), (s_st, max(0, s_st.n - 16))], inc=s_ev)
            E("sync", lambda e, e_i=e_i, f=f, hb=hb: e.dma_start(
                out=h_scr[e_i, f], in_=hb[:]), waits=[(s_ev, vh)], inc=s_st, inc_amt=16)
    vgu_st = s_st.n
    vgu_mm = s_mm.n

    for tc in range(4):
        for e_i in range(2):
            for q in range(4):
                E("sync", lambda e, e_i=e_i, tc=tc, q=q: e.dma_start(
                    out=hch[:, q * 16 * 512:(q + 1) * 16 * 512].rearrange(
                        "p (f t) -> p f t", f=16),
                    in_=h_scr[e_i, q * 16:(q + 1) * 16, :,
                              tc * 512:(tc + 1) * 512].transpose([1, 0, 2])),
                  waits=[(s_st, vgu_st), (s_mm, s_mm.n)], inc=s_h, inc_amt=16)
            vhc = s_h.n
            base_dmm = s_mm.n
            for dt in range(ND):
                wdb = wdst[dt % 2]
                vwd = E("gpsimd", lambda e, e_i=e_i, dt=dt, wdb=wdb: e.dma_start(
                    out=wdb[:].rearrange("p (f c) -> p f c", f=64), in_=wdn[e_i, dt]),
                        waits=[(s_mm, base_dmm + dt - 1 if dt >= 2 else base_dmm),
                               (s_g, s_g.n)], inc=s_w, inc_amt=16)
                pb = pdn[dt % 2]
                for f in range(64):
                    E("tensor", lambda e, f=f, pb=pb, wdb=wdb: e.matmul(
                        out=pb[:], lhsT=wdb[:, f * 128:(f + 1) * 128],
                        rhs=hch[:, f * 512:(f + 1) * 512],
                        start=(f == 0), stop=(f == 63)),
                      waits=([(s_w, vwd), (s_h, vhc), (s_g, s_g.n)] if f == 0 else []),
                      inc=(s_mm if f == 63 else None))
                vdm = s_mm.n
                if e_i == 0:
                    E("vector", lambda e, dt=dt, tc=tc, pb=pb: e.tensor_tensor(
                        out=moetc[:, dt * 512:(dt + 1) * 512], in0=pb[:],
                        in1=w0b[:, tc * 512:(tc + 1) * 512], op=ALU.mult),
                      waits=[(s_mm, vdm), (s_ev, vgate if (tc == 0 and dt == 0) else 0),
                             (s_st, max(0, s_st.n - 16))], inc=s_g)
                else:
                    vm1 = E("vector", lambda e, dt=dt, tc=tc, pb=pb: e.tensor_tensor(
                        out=mstg[:], in0=pb[:], in1=w1b[:, tc * 512:(tc + 1) * 512],
                        op=ALU.mult), waits=[(s_mm, vdm), (s_g, s_g.n)], inc=s_g)
                    E("vector", lambda e, dt=dt: e.tensor_tensor(
                        out=moetc[:, dt * 512:(dt + 1) * 512],
                        in0=moetc[:, dt * 512:(dt + 1) * 512], in1=mstg[:], op=ALU.add),
                      waits=[(s_g, vm1)], inc=s_g)
        vch = s_g.n
        E("sync", lambda e, tc=tc: e.dma_start(
            out=o_moet.rearrange("p (dt t) -> p dt t", dt=ND)[
                :, :, tc * 512:(tc + 1) * 512],
            in_=moetc[:].rearrange("p (dt t) -> p dt t", dt=ND)),
          waits=[(s_g, vch)], inc=s_st, inc_amt=16)
    E("sync", lambda e: e.nop(), waits=[(s_st, s_st.n)])


def host_inputs_launchE(inputs, rd):
    gw = np.asarray(inputs["L1_ffn_gate_w"])
    gwt = np.ascontiguousarray(
        gw.T[ORIG_OF_NEW].reshape(ND, 128, 2).transpose(1, 0, 2).reshape(128, 2 * ND)
        .astype(np.float32))
    def tile_gu(w):
        wt = w.transpose(0, 2, 1)[:, ORIG_OF_NEW, :]
        return np.ascontiguousarray(
            wt.reshape(2, ND, 128, 64, 128).transpose(0, 3, 2, 1, 4))
    def tile_dn(w):
        wt = w.transpose(0, 2, 1)[:, :, ORIG_OF_NEW]
        return np.ascontiguousarray(
            wt.reshape(2, 64, 128, ND, 128).transpose(0, 3, 2, 1, 4))
    return dict(i_xn1t=rd["o_xn1t"], gwt=gwt,
                ident=np.eye(128, dtype=BF),
                wg=tile_gu(np.asarray(inputs["L1_ffn_w_gate"])),
                wu=tile_gu(np.asarray(inputs["L1_ffn_w_up"])),
                wdn=tile_dn(np.asarray(inputs["L1_ffn_w_down"])),
                onesr=np.ones((1, 128), BF))




def build_launchF(nc, ctx):
    i_xn1t = nc.dram_tensor("i_xn1t", [128, ND * T], BF16, kind="ExternalInput").ap()
    i_x1 = nc.dram_tensor("i_x1", [T, D], F32, kind="ExternalInput").ap()
    i_moet = nc.dram_tensor("i_moet", [128, ND * T], BF16, kind="ExternalInput").ap()
    w1t = nc.dram_tensor("w1t", [20, 128, ND, 128], F32, kind="ExternalInput").ap()
    cwt = nc.dram_tensor("cwt", [20, 128, 4, 20, 128], F32, kind="ExternalInput").ap()
    wat = nc.dram_tensor("wat", [20, 128, 20, 128], F32, kind="ExternalInput").ap()
    wxt = nc.dram_tensor("wxt", [20, 128, 20, 128], F32, kind="ExternalInput").ap()
    w2t = nc.dram_tensor("w2t", [ND, 128, 20, 128], F32, kind="ExternalInput").ap()
    vecs = nc.dram_tensor("vecs", [128, 156], F32, kind="ExternalInput").ap()
    ident = nc.dram_tensor("ident", [128, 128], BF16, kind="ExternalInput").ap()
    cst = nc.dram_tensor("cst", [128, 8], F32, kind="ExternalInput").ap()
    o_out = nc.dram_tensor("o_out", [T, D], F32, kind="ExternalOutput").ap()

    NR = 20
    TP = T + 6
    ar = Arena(nc, 206 * KB)
    h0t = ar.at("h0t", 0, [128, NR * TP], BF16)
    hst = ar.at("hst", 0, [128, NR * T], BF16)
    xn1t = ar.at("xn1t", 84 * KB, [128, ND * T], BF16)
    cvt = ar.at("cvt", 84 * KB, [128, NR * T], BF16)
    sumt = ar.at("sumt", 84 * KB, [128, ND * T], BF16)
    w1s = [ar.at(f"w1s{i}", 164 * KB + i * 4 * KB, [128, ND * 128], BF16) for i in range(2)]
    cws = [ar.at(f"cws{i}", 164 * KB + i * 20 * KB, [128, 80 * 128], BF16) for i in range(2)]
    was = ar.at("was", 192 * KB, [128, 20 * 128], BF16)
    wxs = ar.at("wxs", 197 * KB, [128, 20 * 128], BF16)
    rX = ar.at("rX", 164 * KB, [128, T], F32)
    rY = ar.at("rY", 172 * KB, [128, T], F32)
    rZ = ar.at("rZ", 180 * KB, [128, T], F32)
    rM = ar.at("rM", 188 * KB, [128, T], BF16)
    w2s = [ar.at(f"w2s{i}", 164 * KB + i * 5 * KB, [128, 20 * 128], BF16) for i in range(2)]
    mstg = ar.at("mstg", 174 * KB, [128, T], BF16)
    fstg = ar.at("fstg", 178 * KB, [128, T], F32)
    xtm = ar.at("xtm", 186 * KB, [128, D], F32)
    otok = ar.at("otok", 194 * KB, [128, D], BF16)
    svec = ar.at("svec", 205 * KB, [128, 156], F32)
    sid = ar.at("sid", 205 * KB + 640, [128, 128], BF16)
    scst = ar.at("scst", 205 * KB + 920, [128, 8], F32)
    assert 205 * KB + 952 <= 206 * KB

    ps = Psum(nc)
    pa = ps.at("pa", 0, [128, 2048], F32)
    pb = ps.at("pb", 4, [128, 2048], F32)
    ptr = ps.at("ptr", 0, [128, 1024], BF16)
    ptr2 = ps.at("ptr2", 2, [128, 1024], BF16)

    s_pre = ctx.sem("s_pre"); s_w = ctx.sem("s_w"); s_mm = ctx.sem("s_mm")
    s_ev = ctx.sem("s_ev"); s_st = ctx.sem("s_st"); s_x = ctx.sem("s_x")
    s_tp = ctx.sem("s_tp")
    E = ctx.emit

    E("sync", lambda e: e.dma_start(out=xn1t[:], in_=i_xn1t), inc=s_pre, inc_amt=16)
    E("sync", lambda e: e.dma_start(out=svec[:], in_=vecs), inc=s_pre, inc_amt=16)
    E("sync", lambda e: e.dma_start(out=sid[:], in_=ident), inc=s_pre, inc_amt=16)
    E("sync", lambda e: e.dma_start(out=scst[:], in_=cst), inc=s_pre, inc_amt=16)
    vpre = s_pre.n

    v2 = E("scalar", lambda e: e.activation(out=svec[:, 116:136], in_=svec[:, 80:100],
                                            func=AF.Abs), waits=[(s_pre, vpre)], inc=s_ev)
    v3 = E("scalar", lambda e: e.activation(out=svec[:, 136:156], in_=svec[:, 116:136],
                                            func=AF.Exp, scale=-1.0),
           waits=[(s_ev, v2)], inc=s_ev)
    v4 = E("scalar", lambda e: e.activation(out=svec[:, 116:136], in_=svec[:, 136:156],
                                            func=AF.Ln, bias=scst[:, 2:3]),
           waits=[(s_ev, v3)], inc=s_ev)
    v5 = E("scalar", lambda e: e.activation(out=svec[:, 136:156], in_=svec[:, 80:100],
                                            func=AF.Relu), waits=[(s_ev, v4)], inc=s_ev)
    v6 = E("vector", lambda e: e.tensor_tensor(out=svec[:, 116:136], in0=svec[:, 116:136],
                                               in1=svec[:, 136:156], op=ALU.add),
           waits=[(s_ev, v5)], inc=s_ev)
    vspl = E("vector", lambda e: e.tensor_scalar_mul(out=svec[:, 116:136],
                                                     in0=svec[:, 116:136], scalar1=-8.0),
             waits=[(s_ev, v6)], inc=s_ev)

    vz = E("vector", lambda e: e.memset(h0t[:], 0.0), inc=s_ev)
    base_mm = s_mm.n
    for rt in range(NR):
        wb = w1s[rt % 2]
        E("gpsimd", lambda e, rt=rt, wb=wb: e.dma_start(
            out=wb[:].rearrange("p (k c) -> p k c", k=ND), in_=w1t[rt]),
          waits=[(s_mm, base_mm + rt - 1 if rt >= 2 else base_mm), (s_pre, vpre)],
          inc=s_w, inc_amt=16)
        vw = s_w.n
        for k in range(ND):
            for c in range(4):
                E("tensor", lambda e, k=k, c=c, wb=wb: e.matmul(
                    out=pa[:, c * 512:(c + 1) * 512], lhsT=wb[:, k * 128:(k + 1) * 128],
                    rhs=xn1t[:, k * T + c * 512: k * T + (c + 1) * 512],
                    start=(k == 0), stop=(k == ND - 1)),
                  waits=([(s_w, vw), (s_ev, s_ev.n)] if (k == 0 and c == 0) else []),
                  inc=(s_mm if (k == ND - 1 and c == 3) else None))
        vm = s_mm.n
        E("scalar", lambda e, rt=rt: e.activation(
            out=h0t[:, rt * TP + 3: rt * TP + 3 + T], in_=pa[:], func=AF.Silu,
            bias=svec[:, rt:rt + 1]),
          waits=[(s_mm, vm), (s_ev, max(vz, vspl))], inc=s_ev)
    vh0 = s_ev.n

    base_mm = s_mm.n
    for ot in range(NR):
        pp = pa if ot % 2 == 0 else pb
        wb = cws[ot % 2]
        E("gpsimd", lambda e, ot=ot, wb=wb: e.dma_start(
            out=wb[:].rearrange("p (tap it c) -> p tap it c", tap=4, it=20),
            in_=cwt[ot]),
          waits=[(s_mm, base_mm + ot - 1 if ot >= 2 else base_mm)], inc=s_w, inc_amt=16)
        vw = s_w.n
        first = True
        for it in range(NR):
            for tap in range(4):
                idx = tap * NR + it
                for cc in range(4):
                    E("tensor", lambda e, ot=ot, it=it, tap=tap, cc=cc, pp=pp, wb=wb,
                      idx=idx: e.matmul(
                        out=pp[:, cc * 512:(cc + 1) * 512],
                        lhsT=wb[:, idx * 128:(idx + 1) * 128],
                        rhs=h0t[:, it * TP + 2 * tap + cc * 512:
                                it * TP + 2 * tap + cc * 512 + 512],
                        start=(it == 0 and tap == 0), stop=(it == NR - 1 and tap == 3)),
                      waits=([(s_w, vw), (s_ev, s_ev.n)] if first else []),
                      inc=(s_mm if (it == NR - 1 and tap == 3 and cc == 3) else None))
                    first = False
        vm = s_mm.n
        E("vector", lambda e, ot=ot, pp=pp: e.tensor_scalar(
            out=cvt[:, ot * T:(ot + 1) * T], in0=pp[:],
            scalar1=svec[:, 20 + ot:21 + ot], scalar2=None, op0=ALU.add),
          waits=[(s_mm, vm), (s_ev, vh0)], inc=s_ev)
    vcv = s_ev.n

    base_mm = s_mm.n
    for rt in range(NR):
        E("gpsimd", lambda e, rt=rt: e.dma_start(
            out=was[:].rearrange("p (k c) -> p k c", k=NR), in_=wat[rt]),
          waits=[(s_mm, base_mm + 2 * rt - 2 if rt >= 1 else base_mm),
                 (s_ev, s_ev.n)], inc=s_w, inc_amt=16)
        E("gpsimd", lambda e, rt=rt: e.dma_start(
            out=wxs[:].rearrange("p (k c) -> p k c", k=NR), in_=wxt[rt]),
          inc=s_w, inc_amt=16)
        vw = s_w.n
        for k in range(NR):
            for c in range(4):
                E("tensor", lambda e, k=k, c=c: e.matmul(
                    out=pa[:, c * 512:(c + 1) * 512], lhsT=was[:, k * 128:(k + 1) * 128],
                    rhs=cvt[:, k * T + c * 512: k * T + (c + 1) * 512],
                    start=(k == 0), stop=(k == NR - 1)),
                  waits=([(s_w, vw), (s_ev, s_ev.n)] if (k == 0 and c == 0) else []),
                  inc=(s_mm if (k == NR - 1 and c == 3) else None))
        for k in range(NR):
            for c in range(4):
                E("tensor", lambda e, k=k, c=c: e.matmul(
                    out=pb[:, c * 512:(c + 1) * 512], lhsT=wxs[:, k * 128:(k + 1) * 128],
                    rhs=cvt[:, k * T + c * 512: k * T + (c + 1) * 512],
                    start=(k == 0), stop=(k == NR - 1)),
                  waits=([(s_ev, s_ev.n)] if (k == 0 and c == 0) else []),
                  inc=(s_mm if (k == NR - 1 and c == 3) else None))
        vm = s_mm.n
        va1 = E("scalar", lambda e, rt=rt: e.activation(out=rX[:], in_=pa[:],
                                                        func=AF.Sigmoid,
                                                        bias=svec[:, 40 + rt:41 + rt]),
                waits=[(s_mm, vm), (s_ev, s_ev.n)], inc=s_ev)
        va2 = E("scalar", lambda e, rt=rt: e.activation(out=rZ[:], in_=rX[:], func=AF.Exp,
                                                        scale=svec[:, 116 + rt:117 + rt]),
                waits=[(s_ev, va1)], inc=s_ev)
        va3 = E("vector", lambda e: e.tensor_tensor(out=rX[:], in0=rZ[:], in1=rZ[:],
                                                    op=ALU.mult),
                waits=[(s_ev, va2)], inc=s_ev)
        va4 = E("scalar", lambda e: e.activation(out=rM[:], in_=rX[:], func=AF.Sqrt,
                                                 scale=-1.0, bias=scst[:, 2:3]),
                waits=[(s_ev, va3)], inc=s_ev)
        vb1 = E("scalar", lambda e, rt=rt: e.activation(out=rY[:], in_=pb[:],
                                                        func=AF.Sigmoid,
                                                        bias=svec[:, 60 + rt:61 + rt]),
                waits=[(s_mm, vm), (s_ev, va1)], inc=s_ev)
        vb2 = E("vector", lambda e, rt=rt: e.tensor_tensor(
            out=rY[:], in0=rY[:], in1=cvt[:, rt * T:(rt + 1) * T], op=ALU.mult),
                waits=[(s_ev, vb1)], inc=s_ev)
        vb3 = E("vector", lambda e: e.tensor_tensor(out=rY[:], in0=rY[:], in1=rM[:],
                                                    op=ALU.mult),
                waits=[(s_ev, max(vb2, va4))], inc=s_ev)
        E("vector", lambda e, rt=rt: e.tensor_tensor_scan(
            out=hst[:, rt * T:(rt + 1) * T], data0=rZ[:], data1=rY[:], initial=0.0,
            op0=ALU.mult, op1=ALU.add),
          waits=[(s_ev, vb3)], inc=s_ev)
    vrg = s_ev.n

    base_mm = s_mm.n
    for dt in range(ND):
        pp = pa if dt % 2 == 0 else pb
        wb = w2s[dt % 2]
        E("gpsimd", lambda e, dt=dt, wb=wb: e.dma_start(
            out=wb[:].rearrange("p (k c) -> p k c", k=NR), in_=w2t[dt]),
          waits=[(s_mm, base_mm + dt - 1 if dt >= 2 else base_mm), (s_ev, vrg)],
          inc=s_w, inc_amt=16)
        vw = s_w.n
        E("sync", lambda e, dt=dt: e.dma_start(out=mstg[:],
                                               in_=i_moet[:, dt * T:(dt + 1) * T]),
          waits=[(s_ev, s_ev.n)], inc=s_x, inc_amt=16)
        vmo = s_x.n
        for k in range(NR):
            for c in range(4):
                E("tensor", lambda e, k=k, c=c, pp=pp, wb=wb: e.matmul(
                    out=pp[:, c * 512:(c + 1) * 512], lhsT=wb[:, k * 128:(k + 1) * 128],
                    rhs=hst[:, k * T + c * 512: k * T + (c + 1) * 512],
                    start=(k == 0), stop=(k == NR - 1)),
                  waits=([(s_w, vw), (s_ev, s_ev.n)] if (k == 0 and c == 0) else []),
                  inc=(s_mm if (k == NR - 1 and c == 3) else None))
        vm = s_mm.n
        vf1 = E("vector", lambda e, dt=dt, pp=pp: e.tensor_scalar(
            out=fstg[:], in0=pp[:], scalar1=svec[:, 100 + dt:101 + dt], scalar2=None,
            op0=ALU.add), waits=[(s_mm, vm)], inc=s_ev)
        E("vector", lambda e, dt=dt: e.tensor_tensor(
            out=sumt[:, dt * T:(dt + 1) * T], in0=fstg[:], in1=mstg[:], op=ALU.add),
          waits=[(s_ev, vf1), (s_x, vmo)], inc=s_ev)
    vsum = s_ev.n

    for m in range(NT):
        E("sync", lambda e, m=m: e.dma_start(out=xtm[:], in_=i_x1[m * 128:(m + 1) * 128, :]),
          waits=[(s_st, s_st.n)], inc=s_x, inc_amt=16)
        vx = s_x.n
        for half in range(2):
            pt = ptr if half == 0 else ptr2
            for kk in range(8):
                dt = half * 8 + kk
                E("tensor", lambda e, dt=dt, kk=kk, m=m, pt=pt: e.transpose(
                    out=pt[:, kk * 128:(kk + 1) * 128],
                    in_=sumt[:, dt * T + m * 128: dt * T + (m + 1) * 128],
                    identity=sid[:]),
                  waits=([(s_ev, vsum if m == 0 and half == 0 else s_ev.n),
                          (s_tp, s_tp.n)] if kk == 0 else []),
                  inc=(s_tp if kk == 7 else None))
            vtp = s_tp.n
            E("vector", lambda e, half=half, pt=pt: e.tensor_copy(
                out=otok[:, half * 1024:(half + 1) * 1024], in_=pt[:]),
              waits=[(s_tp, vtp)], inc=s_ev)
        vot = s_ev.n
        vfin = E("vector", lambda e: e.tensor_tensor(out=xtm[:], in0=xtm[:], in1=otok[:],
                                                     op=ALU.add),
                 waits=[(s_ev, vot), (s_x, vx)], inc=s_ev)
        E("sync", lambda e, m=m: e.dma_start(out=o_out[m * 128:(m + 1) * 128, :],
                                             in_=xtm[:]),
          waits=[(s_ev, vfin)], inc=s_st, inc_amt=16)
    E("sync", lambda e: e.nop(), waits=[(s_st, s_st.n)])


def host_inputs_launchF(inputs, rd, re_):
    def tile_in_out(w, nin, nout):
        wt = np.ascontiguousarray(w.T)
        return np.ascontiguousarray(
            wt.reshape(nin, 128, nout, 128).transpose(2, 1, 0, 3))
    w1 = np.asarray(inputs["L1_attn_w1"])
    w1p = w1[:, ORIG_OF_NEW]
    w1t = np.ascontiguousarray(
        w1p.T.reshape(ND, 128, 20, 128).transpose(2, 1, 0, 3))
    cw = np.asarray(inputs["L1_attn_conv_w"])
    cwt = np.ascontiguousarray(
        cw.transpose(1, 0, 2).reshape(20, 128, 20, 128, 4).transpose(2, 1, 4, 0, 3))
    wa = np.asarray(inputs["L1_attn_wa"])
    wx = np.asarray(inputs["L1_attn_wx"])
    wat = tile_in_out(wa, 20, 20)
    wxt = tile_in_out(wx, 20, 20)
    w2 = np.asarray(inputs["L1_attn_w2"])
    w2p = w2[ORIG_OF_NEW]
    w2t = np.ascontiguousarray(
        w2p.T.reshape(20, 128, ND, 128).transpose(2, 1, 0, 3))
    def fmcols(v, n):
        return np.asarray(v, np.float32).reshape(n, 128).T
    vecs = np.zeros((128, 156), np.float32)
    vecs[:, 0:20] = fmcols(inputs["L1_attn_b1"], 20)
    vecs[:, 20:40] = fmcols(inputs["L1_attn_conv_b"], 20)
    vecs[:, 40:60] = fmcols(inputs["L1_attn_ba"], 20)
    vecs[:, 60:80] = fmcols(inputs["L1_attn_bx"], 20)
    vecs[:, 80:100] = fmcols(inputs["L1_attn_lam"], 20)
    vecs[:, 100:116] = fmcols(np.asarray(inputs["L1_attn_b2"])[ORIG_OF_NEW], ND)
    return dict(i_xn1t=rd["o_xn1t"], i_x1=rd["o_x1"], i_moet=re_["o_moet"],
                w1t=w1t, cwt=cwt, wat=wat, wxt=wxt, w2t=w2t,
                vecs=np.ascontiguousarray(vecs), ident=np.eye(128, dtype=BF),
                cst=_consts())


def kernel(**inputs):
    inputs = {k: np.asarray(v) for k, v in inputs.items()}
    NCORE = 8

    def run(build_fn, in_maps):
        return build_and_run(build_fn, in_maps).results

    insA0 = host_inputs_launchA(inputs, 0)
    mapsA = [insA0] + [dict(insA0, x=np.ascontiguousarray(
        np.asarray(inputs["L0_attn_x"][c])[:, ORIG_OF_NEW])) for c in range(1, NCORE)]
    ra = run(build_launchA, mapsA)
    insB0 = host_inputs_launchB(inputs, ra[0])
    mapsB = [insB0] + [dict(insB0, i_qt=ra[c]["o_qt"], i_kt=ra[c]["o_kt"],
                            i_v=ra[c]["o_v"]) for c in range(1, NCORE)]
    rb = run(build_launchB, mapsB)
    insC0 = host_inputs_launchC(inputs, ra[0])
    mapsC = [insC0] + [dict(insC0, i_xn0t=ra[c]["o_xn0t"]) for c in range(1, NCORE)]
    rc = run(build_launchC, mapsC)
    insD0 = host_inputs_launchD(inputs, 0, rb[0], rc[0])
    mapsD = [insD0] + [dict(insD0,
                            i_h=rc[c]["o_h"], i_attnt=rb[c]["o_attnt"],
                            x=np.ascontiguousarray(
                                np.asarray(inputs["L0_attn_x"][c])[:, ORIG_OF_NEW]))
                       for c in range(1, NCORE)]
    rd = run(build_launchD, mapsD)
    insE0 = host_inputs_launchE(inputs, rd[0])
    mapsE = [insE0] + [dict(insE0, i_xn1t=rd[c]["o_xn1t"]) for c in range(1, NCORE)]
    re_ = run(build_launchE, mapsE)
    insF0 = host_inputs_launchF(inputs, rd[0], re_[0])
    mapsF = [insF0] + [dict(insF0, i_xn1t=rd[c]["o_xn1t"], i_x1=rd[c]["o_x1"],
                            i_moet=re_[c]["o_moet"]) for c in range(1, NCORE)]
    rf = run(build_launchF, mapsF)

    out = np.stack([np.asarray(rf[c]["o_out"])[:, NEW_OF_ORIG] for c in range(NCORE)])
    return out.astype(np.float32)
